# version 07
# baseline (speedup 1.0000x reference)
import numpy as np
import ml_dtypes

import concourse.bass as bass
import concourse.mybir as mybir
from concourse.bass_utils import run_bass_kernel_spmd

F32 = mybir.dt.float32
BF16 = mybir.dt.bfloat16
AF = mybir.ActivationFunctionType
ALU = mybir.AluOpType
AX = mybir.AxisListType

S_TOK = 2048
D = 1024
H = 16
E = 8
NT = 16
NO = 8
NI = 8
NKT = 16
NF = 32
CSHIFT = 16.0
EPS = 1e-5
VW = H * 65


class Ctr:
    def __init__(self, nc, name):
        self.guard = nc.semaphore(name)
        self.h = self.guard.__enter__()
        self.n = 0

    def inc(self, instr, k=1):
        instr.then_inc(self.h, k)
        self.n += k
        return self.n


def build(stop_after=None, dbg=None):
    nc = bass.Bass("TRN2", target_bir_lowering=False)

    xin = nc.dram_tensor("x", [128, NT * D], F32, kind="ExternalInput")
    wqd = nc.dram_tensor("wqT", [128, 8192], F32, kind="ExternalInput")
    wkd = nc.dram_tensor("wkT", [128, 8192], F32, kind="ExternalInput")
    wvd = nc.dram_tensor("wvT", [128, 8192], F32, kind="ExternalInput")
    wod = nc.dram_tensor("woT", [128, 8192], F32, kind="ExternalInput")
    bqd = nc.dram_tensor("bq", [128, 8], F32, kind="ExternalInput")
    bkd = nc.dram_tensor("bk", [128, 8], F32, kind="ExternalInput")
    bod = nc.dram_tensor("bo", [128, 8], F32, kind="ExternalInput")
    bvd = nc.dram_tensor("bv_row", [1, D], F32, kind="ExternalInput")
    gwd = nc.dram_tensor("gwT", [128, 64], F32, kind="ExternalInput")
    w1d = nc.dram_tensor("w1h", [E, 128, 32768], F32, kind="ExternalInput")
    w2d = nc.dram_tensor("w2h", [E, 128, 32768], F32, kind="ExternalInput")
    cosd = nc.dram_tensor("cos2", [128, S_TOK], F32, kind="ExternalInput")
    sind = nc.dram_tensor("sin2", [128, S_TOK], F32, kind="ExternalInput")
    identbd = nc.dram_tensor("identb", [128, 128], BF16, kind="ExternalInput")
    identfd = nc.dram_tensor("identf", [128, 128], F32, kind="ExternalInput")
    ptd = nc.dram_tensor("PT", [128, 128], BF16, kind="ExternalInput")
    outd = nc.dram_tensor("out", [128, NT * D], F32, kind="ExternalOutput")

    def sball(name, shape, dt, side=None):
        g = nc.sbuf_tensor(name + "_sb", shape, dt, side=side)
        tns = g.__enter__()
        return tns, g

    def psall(name, shape, dt):
        g = nc.psum_tensor(name, shape, dt)
        tns = g.__enter__()
        return tns, g

    def free(*guards):
        for g in guards:
            g.__exit__(None, None, None)

    t, v, s, g_, sy = nc.tensor, nc.vector, nc.scalar, nc.gpsimd, nc.sync
    dxc = Ctr(nc, "dxc")
    dxx = Ctr(nc, "dxx")
    dxo = Ctr(nc, "dxo")
    dxr = Ctr(nc, "dxr")
    dwg = Ctr(nc, "dwg")
    dwa = Ctr(nc, "dwa")
    dwo = Ctr(nc, "dwo")
    dw1s = [Ctr(nc, "dw1a"), Ctr(nc, "dw1b")]
    dw2s = [Ctr(nc, "dw2a"), Ctr(nc, "dw2b"), Ctr(nc, "dw2c")]
    ts = Ctr(nc, "ts")
    vs = Ctr(nc, "vs")
    ss = Ctr(nc, "ss")

    seen = {}

    def W(eng, key, ctr, val):
        if val <= 0:
            return
        cur = seen.get((key, id(ctr)), 0)
        if val > cur:
            eng.wait_ge(ctr.h, val)
            seen[(key, id(ctr))] = val

    wt = lambda c, n: W(t, "t", c, n)
    wv_ = lambda c, n: W(v, "v", c, n)
    ws = lambda c, n: W(s, "s", c, n)
    wg = lambda c, n: W(g_, "g", c, n)
    wsy = lambda c, n: W(sy, "sy", c, n)

    identb, _ = sball("identb", [128, 128], BF16)
    identf, _ = sball("identf", [128, 128], F32)
    ptb, _ = sball("ptb", [128, 128], BF16)
    bq, _ = sball("bq", [128, 8], F32)
    bk, _ = sball("bk", [128, 8], F32)
    bo, _ = sball("bo", [128, 8], F32)
    bv_row, _ = sball("bv_row", [1, D], F32)
    gwf, _ = sball("gwf", [128, 64], F32)
    gwhi, _ = sball("gwhi", [128, 64], BF16)
    gwlo, _ = sball("gwlo", [128, 64], BF16)
    onesb, _ = sball("onesb", [128, 128], BF16)
    onesf, _ = sball("onesf", [1, 128], F32)
    csh, _ = sball("csh", [128, 1], F32)
    epsc, _ = sball("epsc", [128, 1], F32)
    combine, _ = sball("combine", [128, 128], F32)
    bvb, _ = sball("bvb", [128, D], F32)
    sums, _ = sball("sums", [128, NT], F32)
    sumsq, _ = sball("sumsq", [128, NT], F32)
    meanc, _ = sball("meanc", [128, NT], F32)
    negv, _ = sball("negv", [128, NT], F32)
    lnvv, _ = sball("lnvv", [128, NT], F32)
    rstd, _ = sball("rstd", [128, NT], F32)
    sqscr, _ = sball("sqscr", [128, D], F32)
    ga_eq1, _ = sball("ga_eq1", [128, 8], F32)
    ga_eq2, _ = sball("ga_eq2", [128, 8], F32)
    ga_gm, _ = sball("ga_gm", [128, 8], F32)
    ga_v1, _ = sball("ga_v1", [128, 1], F32)
    ga_v2, _ = sball("ga_v2", [128, 1], F32)
    ga_t1, _ = sball("ga_t1", [128, 1], F32)
    ga_g1, _ = sball("ga_g1", [128, 1], F32)
    ga_g2, _ = sball("ga_g2", [128, 1], F32)
    gate_all, _ = sball("gate_all", [128, 128], F32)


    def dbg_stop(tag, src_ap, waits=()):
        for eng_w, c, n in waits:
            wsy(c, n)
            wg(c, n)
        for c in (dxc, dxx, dxo, dxr, dwg, dwa, dwo, *dw1s, *dw2s):
            if c.n:
                g_.wait_ge(c.h, c.n)
        if src_ap.dtype == F32:
            sy.dma_start(out=outd[0:src_ap.shape[0], 0:src_ap.shape[1]],
                         in_=src_ap).then_inc(dxc.h, 16)
            dxc.n += 16
            sy.wait_ge(dxc.h, dxc.n)
        else:
            g_.dma_start(out=outd[0:src_ap.shape[0], 0:src_ap.shape[1]],
                         in_=src_ap).then_inc(dwg.h, 16)
            dwg.n += 16
            g_.wait_ge(dwg.h, dwg.n)

    for dst, src in ((identb, identbd), (identf, identfd), (ptb, ptd),
                     (bq, bqd), (bk, bkd), (bo, bod), (bv_row, bvd),
                     (gwf, gwd)):
        sy.dma_start(out=dst[:, :], in_=src[:, :]).then_inc(dxc.h, 16)
        dxc.n += 16
    dx_consts = dxc.n
    g_.dma_start(out=gwhi[:, :], in_=gwd[:, :]).then_inc(dwg.h, 16); dwg.n += 16
    gwhi_ld = dwg.n

    v.memset(onesb[:, :], 1.0)
    v.memset(onesf[:, :], 1.0)
    v.memset(sumsq[:, :], 0.0)
    v.memset(epsc[:, :], EPS)
    vs.inc(v.memset(csh[:, :], -CSHIFT))
    memset_done = vs.n
    wv_(dxc, dx_consts)
    wv_(dwg, gwhi_ld)
    vs.inc(v.tensor_tensor(gwlo[:, :], gwf[:, :], gwhi[:, :], ALU.subtract))
    gw_done = vs.n

    pbv, pbv_g = psall("pbv", [128, D], F32)
    wt(dxc, dx_consts)
    wt(vs, memset_done)
    t.matmul(pbv[:, 0:512], onesf[0:1, :], bv_row[0:1, 0:512], start=True, stop=True)
    ts.inc(t.matmul(pbv[:, 512:1024], onesf[0:1, :], bv_row[0:1, 512:1024],
                    start=True, stop=True))
    wv_(ts, ts.n)
    vs.inc(v.tensor_copy(bvb[:, :], pbv[:, :]))
    bvb_done = vs.n
    free(pbv_g)

    ln1T, ln1T_g = sball("ln1T", [128, NI * S_TOK], BF16, side="right")
    cos2, cos2_g = sball("cos2", [128, S_TOK], F32, side="right")
    sin2, sin2_g = sball("sin2", [128, S_TOK], F32, side="right")
    wbufA, wbufA_g = sball("wbufA", [128, 8192], BF16, side="right")
    wbufB, wbufB_g = sball("wbufB", [128, 8192], BF16, side="right")
    wbuf = [wbufA, wbufB]

    sy.dma_start(out=cos2[:, :], in_=cosd[:, :]).then_inc(dxr.h, 16); dxr.n += 16
    sy.dma_start(out=sin2[:, :], in_=sind[:, :]).then_inc(dxr.h, 16); dxr.n += 16
    rope_tab = dxr.n
    g_.dma_start(out=wbuf[0][:, :], in_=wqd[:, :]).then_inc(dwa.h, 16); dwa.n += 16
    g_.dma_start(out=wbuf[1][:, :], in_=wkd[:, :]).then_inc(dwa.h, 16); dwa.n += 16
    wq_ld = wk_ld = dwa.n

    xtmp, xtmp_g = sball("xtmp", [128, NT * D], F32)

    xload = []
    for c4 in range(4):
        sy.dma_start(out=xtmp[:, 4096 * c4:4096 * (c4 + 1)],
                     in_=xin[:, 4096 * c4:4096 * (c4 + 1)]).then_inc(dxx.h, 16)
        dxx.n += 16
        xload.append(dxx.n)

    def emit_ln_tr(src_fn, wait_src, dstT, stage, ptrX,
                   loT=None, lostage=None, lotmpf=None):
        stats_v = [0] * NT
        stats_s = [0] * NT
        stage_free_t = [0, 0]
        stage_free_v = [0, 0]
        lostage_free_t = [0, 0]
        tr_evac = [0, 0]

        def stats(tt):
            wait_src(tt)
            src = src_fn(tt)
            i_sum = vs.inc(v.reduce_sum(sums[:, tt:tt + 1], src, axis=AX.X))
            ws(vs, memset_done)
            wait_src(tt, "s")
            i_sq = ss.inc(s.activation(sqscr[:, :], src, AF.Square,
                                       accum_out=sumsq[:, tt:tt + 1]))
            wv_(vs, i_sum)
            i_mean = vs.inc(v.tensor_scalar(meanc[:, tt:tt + 1],
                                            sums[:, tt:tt + 1],
                                            1.0 / D, None, ALU.mult))
            wv_(ss, i_sq)
            i_t = vs.inc(v.tensor_scalar(negv[:, tt:tt + 1], sumsq[:, tt:tt + 1],
                                         -1.0 / D, None, ALU.mult))
            wv_(vs, i_t)
            i_nv = vs.inc(v.scalar_tensor_tensor(
                negv[:, tt:tt + 1], meanc[:, tt:tt + 1], meanc[:, tt:tt + 1],
                negv[:, tt:tt + 1], ALU.mult, ALU.add))
            stats_v[tt] = i_nv
            ws(vs, i_nv)
            i_ln = ss.inc(s.activation(lnvv[:, tt:tt + 1], negv[:, tt:tt + 1],
                                       AF.Ln, bias=epsc[:, 0:1], scale=-1.0))
            ws(ss, i_ln)
            i_rs = ss.inc(s.activation(rstd[:, tt:tt + 1], lnvv[:, tt:tt + 1],
                                       AF.Exp, bias=0.0, scale=-0.5))
            stats_s[tt] = i_rs

        def finish(tt):
            src = src_fn(tt)
            slot = tt % 2
            wv_(ss, stats_s[tt])
            wv_(vs, stats_v[tt])
            wv_(ts, stage_free_t[slot])
            wv_(vs, stage_free_v[slot])
            i_nm = vs.inc(v.tensor_scalar(
                stage[:, D * slot:D * (slot + 1)], src,
                meanc[:, tt:tt + 1], rstd[:, tt:tt + 1],
                ALU.subtract, ALU.mult))
            wt(vs, i_nm)
            wt(vs, tr_evac[0])
            mm = None
            for dt in range(NI):
                mm = t.transpose(
                    ptrX[:, 128 * dt:128 * (dt + 1)],
                    stage[:, D * slot + 128 * dt:D * slot + 128 * (dt + 1)],
                    identb[:, :])
            trm = ts.inc(mm)
            stage_free_t[slot] = trm
            wv_(ts, trm)
            out_ap = bass.AP(dstT, 128 * tt,
                             [[NI * S_TOK, 128], [S_TOK, NI], [1, 128]])
            in_ap = bass.AP(ptrX, 0, [[2048, 128], [128, NI], [1, 128]])
            tr_evac[0] = vs.inc(v.tensor_copy(out_ap, in_ap))
            if loT is not None:
                i_f = vs.inc(v.tensor_scalar(
                    lotmpf[:, :], src, meanc[:, tt:tt + 1], rstd[:, tt:tt + 1],
                    ALU.subtract, ALU.mult))
                wv_(vs, i_f)
                wv_(ts, lostage_free_t[slot])
                i_lo = vs.inc(v.tensor_tensor(
                    lostage[:, D * slot:D * (slot + 1)], lotmpf[:, :],
                    stage[:, D * slot:D * (slot + 1)], ALU.subtract))
                stage_free_v[slot] = i_lo
                wt(vs, i_lo)
                wt(vs, tr_evac[1])
                mm = None
                for dt in range(NI):
                    mm = t.transpose(
                        ptrX[:, 1024 + 128 * dt:1024 + 128 * (dt + 1)],
                        lostage[:, D * slot + 128 * dt:D * slot + 128 * (dt + 1)],
                        identb[:, :])
                trl = ts.inc(mm)
                lostage_free_t[slot] = trl
                wv_(ts, trl)
                out_ap = bass.AP(loT, 128 * tt,
                                 [[NI * S_TOK, 128], [S_TOK, NI], [1, 128]])
                in_ap = bass.AP(ptrX, 1024, [[2048, 128], [128, NI], [1, 128]])
                tr_evac[1] = vs.inc(v.tensor_copy(out_ap, in_ap))

        for tt in range(NT):
            stats(tt)
            if tt > 0:
                finish(tt - 1)
        finish(NT - 1)
        return vs.n

    def wait_x1(tt, eng="v"):
        c = xload[3]
        if eng == "v":
            wv_(dxx, c)
        else:
            ws(dxx, c)

    stage1, stage1_g = sball("stage1", [128, 2 * D], BF16, side="right")
    ptr1, ptr1_g = psall("ptr1", [128, 2048], BF16)
    ln1T_done = emit_ln_tr(lambda tt: xtmp[:, D * tt:D * (tt + 1)], wait_x1,
                           ln1T, stage1, ptr1)
    if stop_after == "ln1":
        dbg_stop("ln1", ln1T[:, :], waits=((wsy, vs, ln1T_done),))
        return nc
    free(ptr1_g, stage1_g, xtmp_g)

    QTr, QTr_g = sball("QTr", [128, NO * S_TOK], BF16)
    KTr, KTr_g = sball("KTr", [128, NO * S_TOK], BF16)
    Vn, Vn_g = sball("Vn", [128, NT * D], BF16)
    qtmp0, qtmp0_g = sball("qtmp0", [128, S_TOK], BF16, side="right")
    qtmp1, qtmp1_g = sball("qtmp1", [128, S_TOK], BF16, side="right")
    qtmps = [qtmp0, qtmp1]
    ctmp, ctmp_g = sball("ctmp", [128, S_TOK], BF16, side="right")
    pqk0, pqk0_g = psall("pqk0", [128, S_TOK], F32)
    pqk1, pqk1_g = psall("pqk1", [128, S_TOK], F32)
    pqks = [pqk0, pqk1]


    def emit_qk(wsel, wld, bias_sb, dstT, init_marks=None):
        marks = init_marks if init_marks is not None else [(0, 0), (0, 0)]
        done = e1 = 0
        for ot in range(NO):
            p = ot % 2
            pqk = pqks[p]
            qtmp = qtmps[p]
            wt(dwa, wld)
            wt(vs, ln1T_done)
            pe1, pm2 = marks[p]
            wt(ss, pe1)
            wt(vs, pm2)
            mm = None
            for i in range(NI):
                lhsT = wbuf[wsel][:, 1024 * i + 128 * ot:1024 * i + 128 * (ot + 1)]
                for qc in range(4):
                    mm = t.matmul(
                        pqk[:, 512 * qc:512 * (qc + 1)], lhsT,
                        ln1T[:, S_TOK * i + 512 * qc:S_TOK * i + 512 * (qc + 1)],
                        start=(i == 0), stop=(i == NI - 1))
            proj = ts.inc(mm)
            ws(ts, proj)
            ws(dxc, dx_consts)
            e1 = ss.inc(s.activation(qtmp[:, :], pqk[:, :], AF.Identity,
                                     bias=bias_sb[:, ot:ot + 1], scale=1.0))
            wt(ss, e1)
            mm = None
            for qc in range(4):
                mm = t.matmul(pqk[:, 512 * qc:512 * (qc + 1)], ptb[:, :],
                              qtmp[:, 512 * qc:512 * (qc + 1)],
                              start=True, stop=True)
            rotmm = ts.inc(mm)
            wv_(ss, e1)
            wv_(dxr, rope_tab)
            e2 = vs.inc(v.tensor_tensor(ctmp[:, :], qtmp[:, :], cos2[:, :],
                                        ALU.mult))
            wv_(ts, rotmm)
            wv_(vs, e2)
            m1 = vs.inc(v.tensor_tensor(pqk[:, :], pqk[:, :], sin2[:, :],
                                        ALU.mult))
            wv_(vs, m1)
            m2 = vs.inc(v.tensor_tensor(dstT[:, S_TOK * ot:S_TOK * (ot + 1)],
                                        ctmp[:, :], pqk[:, :], ALU.add))
            marks[p] = (e1, m2)
            done = m2
        return done, e1, marks

    qdone, q_e1, qk_marks = emit_qk(0, wq_ld, bq, QTr)
    if stop_after == "q":
        dbg_stop("q", QTr[:, :], waits=((None, vs, qdone),))
        return nc
    wg(ts, ts.n)
    g_.dma_start(out=wbuf[0][:, :], in_=wvd[:, :]).then_inc(dwa.h, 16); dwa.n += 16
    wv_ld = dwa.n
    kdone, k_e1, qk_marks = emit_qk(1, wk_ld, bk, KTr, init_marks=qk_marks)
    if stop_after == "k":
        dbg_stop("k", KTr[:, :], waits=((None, vs, kdone),))
        return nc
    free(pqk1_g, pqk0_g)

    pv0, pv0_g = psall("pv0", [128, D], F32)
    pv1, pv1_g = psall("pv1", [128, D], F32)
    pv = [pv0, pv1]
    pv_read = [0, 0]
    for tt in range(NT):
        ping = tt % 2
        wt(dwa, wv_ld)
        wt(vs, ln1T_done)
        wt(vs, kdone)
        wt(ss, k_e1)
        wt(vs, pv_read[ping])
        mm = None
        for i in range(NI):
            lhsT = ln1T[:, S_TOK * i + 128 * tt:S_TOK * i + 128 * (tt + 1)]
            for oc in range(2):
                mm = t.matmul(
                    pv[ping][:, 512 * oc:512 * (oc + 1)], lhsT,
                    wbuf[0][:, 1024 * i + 512 * oc:1024 * i + 512 * (oc + 1)],
                    start=(i == 0), stop=(i == NI - 1))
        vmm = ts.inc(mm)
        wv_(ts, vmm)
        wv_(vs, bvb_done)
        pv_read[ping] = vs.inc(v.tensor_tensor(
            Vn[:, D * tt:D * (tt + 1)], pv[ping][:, :], bvb[:, :], ALU.add))
    v_done = vs.n
    if stop_after == "qkv":
        if dbg == "QTr":
            dbg_stop("qkv", QTr[:, :], waits=((wsy, vs, qdone),))
        elif dbg == "KTr":
            dbg_stop("qkv", KTr[:, :], waits=((wsy, vs, kdone),))
        else:
            dbg_stop("qkv", Vn[:, :], waits=((wsy, vs, v_done),))
        return nc
    free(pv1_g, pv0_g)
    free(ctmp_g, qtmp1_g, qtmp0_g, wbufB_g, wbufA_g, sin2_g, cos2_g, ln1T_g)

    wosbuf, wosbuf_g = sball("wosbuf", [128, 8192], BF16, side="right")
    OTsb, OTsb_g = sball("OTsb", [128, NO * S_TOK], BF16, side="right")
    expbs_g = []
    expbs = []
    for i_ in range(2):
        tns, gg = sball(f"expb{i_}", [128, 1024], BF16, side="right")
        expbs.append(tns)
        expbs_g.append(gg)
    denl, denl_g = sball("denl", [128, 512], F32, side="right")
    invd, invd_g = sball("invd", [128, 512], BF16, side="right")

    wg(ts, ts.n)
    g_.dma_start(out=wosbuf[:, :], in_=wod[:, :]).then_inc(dwo.h, 16); dwo.n += 16
    wo_ld = dwo.n

    srot_g = []
    srot = []
    for i_ in range(2):
        tns, gg = psall(f"pSr{i_}", [128, 1024], F32)
        srot.append(tns)
        srot_g.append(gg)
    pOTp = []
    pOT_g = []
    for i_ in range(2):
        tns, gg = psall(f"pOT{i_}", [128, 512], F32)
        pOTp.append(tns)
        pOT_g.append(gg)
    pDnp = []
    pDn_g = []
    for i_ in range(2):
        tns, gg = psall(f"pDn{i_}", [128, 512], F32)
        pDnp.append(tns)
        pDn_g.append(gg)

    s_read = [0, 0]
    eb_free = [0, 0]
    ot_free = [0, 0]
    dn_free = [0, 0]
    it_idx = 0

    for og in range(NO):
        for qq in range(4):
            qoff = S_TOK * og + 512 * qq
            ping = it_idx % 2
            sc_marks = {}

            def emit_S(kt):
                sl = kt % 2
                wt(ss, s_read[sl])
                if it_idx == 0 and kt == 0:
                    wt(vs, qdone)
                    wt(vs, kdone)
                    wt(vs, v_done)
                mm = None
                for b in range(2):
                    mm = t.matmul(
                        srot[sl][:, 512 * b:512 * (b + 1)],
                        KTr[64 * b:64 * (b + 1),
                            S_TOK * og + 128 * kt:S_TOK * og + 128 * (kt + 1)],
                        QTr[64 * b:64 * (b + 1), qoff:qoff + 512],
                        start=True, stop=True,
                        tile_position=(64 * b, 0))
                sc_marks[kt] = ts.inc(mm)

            emit_S(0)
            for kt in range(NKT):
                sl = kt % 2
                ws(ts, sc_marks[kt])
                ws(vs, memset_done)
                ws(ts, eb_free[sl])
                ee = ss.inc(s.activation(expbs[sl][:, :], srot[sl][:, :],
                                         AF.Exp, bias=csh[:, 0:1], scale=0.125))
                s_read[sl] = ee
                if kt + 1 < NKT:
                    emit_S(kt + 1)
                wt(ss, ee)
                if kt == 0:
                    wt(vs, ot_free[ping])
                    wt(ss, dn_free[ping])
                for b in range(2):
                    mm = t.matmul(
                        pOTp[ping][64 * b:64 * (b + 1), :],
                        Vn[:, D * kt + 128 * og + 64 * b:
                           D * kt + 128 * og + 64 * (b + 1)],
                        expbs[sl][:, 512 * b:512 * (b + 1)],
                        start=(kt == 0), stop=(kt == NKT - 1),
                        tile_position=(0, 64 * b))
                    ts.inc(mm)
                    mm = t.matmul(
                        pDnp[ping][64 * b:64 * (b + 1), :],
                        onesb[:, 0:64],
                        expbs[sl][:, 512 * b:512 * (b + 1)],
                        start=(kt == 0), stop=(kt == NKT - 1),
                        tile_position=(0, 64 * b))
                    eb_free[sl] = ts.inc(mm)
            av_last = ts.n
            ws(ts, av_last)
            dn1 = ss.inc(s.activation(denl[:, :], pDnp[ping][:, :], AF.Ln,
                                      bias=0.0, scale=1.0))
            dn_free[ping] = dn1
            ws(ss, dn1)
            ws(vs, ot_free[1 - ping])
            dn2 = ss.inc(s.activation(invd[:, :], denl[:, :], AF.Exp,
                                      bias=0.0, scale=-1.0))
            wv_(ss, dn2)
            wv_(ts, av_last)
            ot_free[ping] = vs.inc(v.tensor_tensor(
                OTsb[:, qoff:qoff + 512], pOTp[ping][:, :], invd[:, :],
                ALU.mult))
            it_idx += 1
    scores_v = vs.n
    scores_dx = dxo.n
    if stop_after == "scores":
        dbg_stop("scores", OTsb[:, :], waits=((wsy, vs, scores_v), (wsy, dxo, scores_dx)))
        return nc
    free(pDn_g[1], pDn_g[0], pOT_g[1], pOT_g[0], srot_g[1], srot_g[0])
    free(Vn_g, KTr_g, QTr_g)

    x2, x2_g = sball("x2", [128, NT * D], F32)
    attnT, attnT_g = sball("attnT", [128, NO * S_TOK], BF16)

    scores_ts = ts.n
    xfload = []
    wsy(ts, scores_ts)
    for c4 in range(4):
        sy.dma_start(out=x2[:, 4096 * c4:4096 * (c4 + 1)],
                     in_=xin[:, 4096 * c4:4096 * (c4 + 1)]).then_inc(dxx.h, 16)
        dxx.n += 16
        xfload.append(dxx.n)

    po0, po0_g = psall("po0", [128, S_TOK], F32)
    po1, po1_g = psall("po1", [128, S_TOK], F32)
    po = [po0, po1]
    po_read = [0, 0]
    attn_evac = [0] * NO
    for dt in range(NO):
        ping = dt % 2
        wt(dwo, wo_ld)
        wt(vs, scores_v)
        wt(dxo, scores_dx)
        wt(ss, ss.n if dt == 0 else 0)
        wt(ss, po_read[ping])
        mm = None
        for j in range(NO):
            lhsT = wosbuf[:, 1024 * j + 128 * dt:1024 * j + 128 * (dt + 1)]
            for qc in range(4):
                mm = t.matmul(
                    po[ping][:, 512 * qc:512 * (qc + 1)], lhsT,
                    OTsb[:, S_TOK * j + 512 * qc:S_TOK * j + 512 * (qc + 1)],
                    start=(j == 0), stop=(j == NO - 1))
        omm = ts.inc(mm)
        ws(ts, omm)
        po_read[ping] = ss.inc(s.activation(
            attnT[:, S_TOK * dt:S_TOK * (dt + 1)], po[ping][:, :], AF.Identity,
            bias=bo[:, dt:dt + 1], scale=1.0))
        attn_evac[dt] = po_read[ping]
    free(po1_g, po0_g)

    ptr2, ptr2_g = psall("ptr2", [128, 2048], BF16)
    res_marks = [0] * NT
    tr2_evac = [0, 0]
    for tt in range(NT):
        ping = tt % 2
        wt(ss, attn_evac[NO - 1])
        wt(vs, tr2_evac[ping])
        mm = None
        for dt in range(NO):
            mm = t.transpose(
                ptr2[:, 1024 * ping + 128 * dt:1024 * ping + 128 * (dt + 1)],
                attnT[:, S_TOK * dt + 128 * tt:S_TOK * dt + 128 * (tt + 1)],
                identb[:, :])
        trm = ts.inc(mm)
        wv_(ts, trm)
        wv_(dxx, xfload[3])
        in_ap = bass.AP(ptr2, 1024 * ping, [[2048, 128], [128, NI], [1, 128]])
        tr2_evac[ping] = vs.inc(v.tensor_tensor(
            x2[:, D * tt:D * (tt + 1)], x2[:, D * tt:D * (tt + 1)],
            in_ap, ALU.add))
        res_marks[tt] = tr2_evac[ping]
    if stop_after == "oproj":
        dbg_stop("oproj", x2[:, :], waits=((wsy, vs, res_marks[NT - 1]),))
        return nc
    p4_ts_end = ts.n
    free(ptr2_g, attnT_g)
    free(invd_g, denl_g, expbs_g[1], expbs_g[0], OTsb_g, wosbuf_g)

    ln2T, ln2T_g = sball("ln2T", [128, NI * S_TOK], BF16, side="right")
    ln2loT, ln2loT_g = sball("ln2loT", [128, NI * S_TOK], BF16, side="right")
    stage2, stage2_g = sball("stage2", [128, 2 * D], BF16, side="right")
    lostage, lostage_g = sball("lostage", [128, 2 * D], BF16, side="right")
    lotmpf, lotmpf_g = sball("lotmpf", [128, D], F32, side="right")
    gateTs, gateTs_g = sball("gateTs", [8, S_TOK], F32, side="right")

    wv_(ss, ss.n)
    ms2 = vs.inc(v.memset(sumsq[:, :], 0.0))

    def wait_x2(tt, eng="v"):
        c = res_marks[tt]
        if eng == "v":
            wv_(vs, c)
            wv_(vs, ms2)
        else:
            ws(vs, c)
            ws(vs, ms2)

    ptr3, ptr3_g = psall("ptr3", [128, 2048], BF16)
    ln2T_done = emit_ln_tr(lambda tt: x2[:, D * tt:D * (tt + 1)], wait_x2,
                           ln2T, stage2, ptr3,
                           loT=ln2loT, lostage=lostage, lotmpf=lotmpf)

    pgT, pgT_g = psall("pgT", [8, S_TOK], F32)
    wt(vs, ln2T_done)
    wt(vs, gw_done)
    wt(dwg, gwhi_ld)
    mm = None
    combos = [(gwhi, ln2T), (gwhi, ln2loT), (gwlo, ln2T), (gwlo, ln2loT)]
    ncomb = len(combos)
    for i in range(NI):
        for ci, (gwx, lnx) in enumerate(combos):
            first = (i == 0 and ci == 0)
            last = (i == NI - 1 and ci == ncomb - 1)
            for tc in range(4):
                mm = t.matmul(
                    pgT[:, 512 * tc:512 * (tc + 1)], gwx[:, 8 * i:8 * (i + 1)],
                    lnx[:, S_TOK * i + 512 * tc:S_TOK * i + 512 * (tc + 1)],
                    start=first, stop=last)
    gmm = ts.inc(mm)
    ws(ts, gmm)
    gevac = ss.inc(s.copy(gateTs[:, :], pgT[:, :]))
    free(pgT_g)

    pgt, pgt_g = psall("pgt", [128, 128], F32)
    wt(ss, gevac)
    mm = None
    for tt in range(NT):
        mm = t.transpose(pgt[:, 8 * tt:8 * (tt + 1)],
                         gateTs[0:8, 128 * tt:128 * (tt + 1)], identf[0:8, 0:8])
    gtr = ts.inc(mm)
    wv_(ts, gtr)
    gaev = vs.inc(v.tensor_copy(gate_all[:, :], pgt[:, :]))
    free(pgt_g)

    comb_marks = [0] * NT
    for tt in range(NT):
        ga = gate_all[:, 8 * tt:8 * (tt + 1)]
        cb = combine[:, 8 * tt:8 * (tt + 1)]
        wv_(vs, gaev)
        i1 = vs.inc(v.reduce_max(ga_v1[:, 0:1], ga, axis=AX.X))
        wv_(vs, i1)
        i2 = vs.inc(v.tensor_scalar(ga_eq1[:, :], ga, ga_v1[:, 0:1], None,
                                    ALU.is_equal))
        wv_(vs, i2)
        i3 = vs.inc(v.scalar_tensor_tensor(ga_gm[:, :], ga_eq1[:, :], -1e9,
                                           ga, ALU.mult, ALU.add))
        wv_(vs, i3)
        i4 = vs.inc(v.reduce_max(ga_v2[:, 0:1], ga_gm[:, :], axis=AX.X))
        wv_(vs, i4)
        i5 = vs.inc(v.tensor_scalar(ga_eq2[:, :], ga_gm[:, :], ga_v2[:, 0:1],
                                    None, ALU.is_equal))
        i6 = vs.inc(v.tensor_scalar(ga_t1[:, 0:1], ga_v2[:, 0:1], ga_v1[:, 0:1],
                                    None, ALU.subtract))
        ws(vs, i6)
        i7 = ss.inc(s.activation(ga_t1[:, 0:1], ga_t1[:, 0:1], AF.Exp,
                                 bias=0.0, scale=1.0))
        wv_(ss, i7)
        i8 = vs.inc(v.tensor_scalar(ga_t1[:, 0:1], ga_t1[:, 0:1], 1.0, None,
                                    ALU.add))
        wv_(vs, i8)
        i9 = vs.inc(v.reciprocal(ga_g1[:, 0:1], ga_t1[:, 0:1]))
        wv_(vs, i9)
        i10 = vs.inc(v.tensor_scalar(ga_g2[:, 0:1], ga_g1[:, 0:1], -1.0, 1.0,
                                     ALU.mult, ALU.add))
        wv_(vs, i10)
        i11 = vs.inc(v.tensor_scalar(cb, ga_eq1[:, :], ga_g1[:, 0:1], None,
                                     ALU.mult))
        wv_(vs, i11)
        comb_marks[tt] = vs.inc(v.scalar_tensor_tensor(
            cb, ga_eq2[:, :], ga_g2[:, 0:1], cb, ALU.mult, ALU.add))
    comb_done = vs.n
    if stop_after == "ln2":
        if dbg == "comb":
            dbg_stop("ln2", combine[:, :], waits=((wsy, vs, comb_done),))
        else:
            dbg_stop("ln2", ln2T[:, :], waits=((wsy, vs, ln2T_done), (wsy, vs, comb_done)))
        return nc
    free(gateTs_g, lotmpf_g, lostage_g, stage2_g, ln2loT_g, ptr3_g)

    hTb, hTb_g = sball("hTb", [128, NF * 1024], BF16)
    w1r0, w1r0_g = sball("w1r0", [128, 4096], BF16)
    w1r1, w1r1_g = sball("w1r1", [128, 4096], BF16)
    w1r = [w1r0, w1r1]
    w2r0, w2r0_g = sball("w2r0", [128, 2048], BF16)
    w2r1, w2r1_g = sball("w2r1", [128, 2048], BF16)
    w2r2, w2r2_g = sball("w2r2", [128, 2048], BF16)
    w2r = [w2r0, w2r1, w2r2]
    parena, parena_g = psall("parena", [128, 4096], F32)

    N_W1C = 8
    N_W2C = 16
    w1_loaded = {}
    w2_loaded = {}
    w1_cons = {}
    w2_cons = {}
    yevac = [[0] * 8 for _ in range(2)]
    prev_yevac = [0] * 8

    wt(vs, comb_done)
    wt(ss, gevac)
    moe_dma_safe = ts.n
    NTG = 2 * E

    def emit_w1dma(kc):
        if kc >= NTG * N_W1C:
            return
        e_, c_ = kc // N_W1C // 2, kc % N_W1C
        if kc >= 2:
            wg(ts, w1_cons[kc - 2])
        else:
            wg(ts, moe_dma_safe)
        sl = dw1s[kc % 2]
        g_.dma_start(out=w1r[kc % 2][:, :],
                     in_=w1d[e_, :, 4096 * c_:4096 * (c_ + 1)]
                     ).then_inc(sl.h, 16)
        sl.n += 16
        w1_loaded[kc] = sl.n

    def emit_w2dma(kc):
        if kc >= NTG * N_W2C:
            return
        e_, c_ = kc // N_W2C // 2, kc % N_W2C
        if kc >= 3:
            wg(ts, w2_cons[kc - 3])
        else:
            wg(ts, moe_dma_safe)
        sl = dw2s[kc % 3]
        g_.dma_start(out=w2r[kc % 3][:, :],
                     in_=w2d[e_, :, 2048 * c_:2048 * (c_ + 1)]
                     ).then_inc(sl.h, 16)
        sl.n += 16
        w2_loaded[kc] = sl.n

    for kc in range(2):
        emit_w1dma(kc)
    for kc in range(3):
        emit_w2dma(kc)

    for e in range(E):
        for tg in range(2):
            key = (e, tg)
            relu_marks = [0] * NF
            for ft in range(NF):
                ping = ft % 2
                kc = (e * 2 + tg) * N_W1C + ft // 4
                wt(dw1s[kc % 2], w1_loaded[kc])
                wt(vs, ln2T_done)
                if ft >= 2:
                    wt(ss, relu_marks[ft - 2])
                else:
                    wt(vs, prev_yevac[2 * ping])
                    wt(vs, prev_yevac[2 * ping + 1])
                mm = None
                for i in range(NI):
                    lhsT = w1r[kc % 2][:, 1024 * (ft % 4) + 128 * i:
                                       1024 * (ft % 4) + 128 * (i + 1)]
                    for hf in range(2):
                        mm = t.matmul(
                            parena[:, 1024 * ping + 512 * hf:1024 * ping + 512 * (hf + 1)],
                            lhsT,
                            ln2T[:, S_TOK * i + 1024 * tg + 512 * hf:
                                 S_TOK * i + 1024 * tg + 512 * (hf + 1)],
                            start=(i == 0), stop=(i == NI - 1))
                m1 = ts.inc(mm)
                if ft % 4 == 3:
                    w1_cons[kc] = m1
                    emit_w1dma(kc + 2)
                ws(ts, m1)
                relu_marks[ft] = ss.inc(s.activation(
                    hTb[:, 1024 * ft:1024 * (ft + 1)],
                    parena[:, 1024 * ping:1024 * (ping + 1)], AF.Relu))
            relu_all = ss.n

            for dc in range(2):
                for ft in range(NF):
                    kc = (e * 2 + tg) * N_W2C + dc * 8 + ft // 4
                    wt(dw2s[kc % 3], w2_loaded[kc])
                    wt(ss, relu_marks[ft])
                    for tt in range(8):
                        if ft == 0:
                            if dc == 0:
                                if tt >= 4:
                                    wt(vs, prev_yevac[tt])
                                else:
                                    wt(ss, relu_marks[NF - 2])
                                    wt(ss, relu_marks[NF - 1])
                            else:
                                wt(vs, yevac[0][tt])
                        mm = t.matmul(
                            parena[:, 512 * tt:512 * (tt + 1)],
                            hTb[:, 1024 * ft + 128 * tt:1024 * ft + 128 * (tt + 1)],
                            w2r[kc % 3][:, 512 * (ft % 4):512 * (ft % 4) + 512],
                            start=(ft == 0), stop=(ft == NF - 1))
                    m2 = ts.inc(mm)
                    if ft % 4 == 3:
                        w2_cons[kc] = m2
                        emit_w2dma(kc + 3)
                    if ft == NF - 1:
                        mm2_stop = m2
                for tt in range(8):
                    tgl = 8 * tg + tt
                    wv_(ts, mm2_stop)
                    wv_(vs, comb_done)
                    wv_(vs, res_marks[tgl])
                    yevac[dc][tt] = vs.inc(v.scalar_tensor_tensor(
                        x2[:, D * tgl + 512 * dc:D * tgl + 512 * (dc + 1)],
                        parena[:, 512 * tt:512 * (tt + 1)],
                        combine[:, 8 * tgl + e:8 * tgl + e + 1],
                        x2[:, D * tgl + 512 * dc:D * tgl + 512 * (dc + 1)],
                        ALU.mult, ALU.add))
            prev_yevac = list(yevac[1])
    moe_done = vs.n

    for c4 in range(4):
        wsy(vs, moe_done)
        sy.dma_start(out=outd[:, 4096 * c4:4096 * (c4 + 1)],
                     in_=x2[:, 4096 * c4:4096 * (c4 + 1)]).then_inc(dxc.h, 16)
        dxc.n += 16
    sy.wait_ge(dxc.h, dxc.n)

    free(parena_g, w2r2_g, w2r1_g, w2r0_g, w1r1_g, w1r0_g, hTb_g)
    free(ln2T_g)
    free(x2_g)
    return nc


def _host_layouts(inputs):
    ins = {k: np.asarray(v) for k, v in inputs.items()}

    def wlay(w):
        wT = np.ascontiguousarray(ins[w].T)
        r = wT.reshape(8, 128, wT.shape[1])
        return np.ascontiguousarray(r.transpose(1, 0, 2)).reshape(128, -1)

    maps = {}
    maps["wqT"] = wlay("L0_attn_wq")
    maps["wkT"] = wlay("L0_attn_wk")
    maps["wvT"] = wlay("L0_attn_wv")
    maps["woT"] = wlay("L0_attn_wo")
    maps["bq"] = np.ascontiguousarray(ins["L0_attn_bq"].reshape(8, 128).T)
    maps["bk"] = np.ascontiguousarray(ins["L0_attn_bk"].reshape(8, 128).T)
    maps["bo"] = np.ascontiguousarray(ins["L0_attn_bo"].reshape(8, 128).T)
    maps["bv_row"] = np.ascontiguousarray(ins["L0_attn_bv"].reshape(1, D))
    gwT = np.ascontiguousarray(ins["L0_ffn_gate_w"].T)
    maps["gwT"] = np.ascontiguousarray(
        gwT.reshape(8, 128, 8).transpose(1, 0, 2)).reshape(128, 64)

    w1 = ins["L0_ffn_w1"]
    w1h = np.empty((E, 128, 32768), np.float32)
    for e in range(E):
        a = w1[e].reshape(32, 128, 8, 128)
        w1h[e] = np.ascontiguousarray(a.transpose(3, 0, 2, 1)).reshape(128, 32768)
    maps["w1h"] = w1h
    w2 = ins["L0_ffn_w2"]
    w2h = np.empty((E, 128, 32768), np.float32)
    for e in range(E):
        b = w2[e].reshape(2, 512, 32, 128)
        w2h[e] = np.ascontiguousarray(b.transpose(3, 0, 2, 1)).reshape(128, 32768)
    maps["w2h"] = w2h

    j = np.arange(32, dtype=np.float64)
    freqs = ROPE_BASE_F ** (2.0 * j / 64.0)
    tpos = np.arange(S_TOK, dtype=np.float64)
    ang = tpos[None, :] / freqs[:, None]
    cos1 = np.repeat(np.cos(ang), 2, axis=0)
    sin1 = np.repeat(np.sin(ang), 2, axis=0)
    maps["cos2"] = np.ascontiguousarray(
        np.concatenate([cos1, cos1], axis=0)).astype(np.float32)
    maps["sin2"] = np.ascontiguousarray(
        np.concatenate([sin1, sin1], axis=0)).astype(np.float32)

    maps["identb"] = np.eye(128, dtype=ml_dtypes.bfloat16)
    maps["identf"] = np.eye(128, dtype=np.float32)
    pt = np.zeros((128, 128), np.float32)
    for jj in range(64):
        pt[2 * jj + 1, 2 * jj] = -1.0
        pt[2 * jj, 2 * jj + 1] = 1.0
    maps["PT"] = pt.astype(ml_dtypes.bfloat16)
    return maps


ROPE_BASE_F = 10000.0


def kernel(**inputs):
    shared = _host_layouts(inputs)
    x = np.asarray(inputs["L0_attn_x"])
    in_maps = []
    for c in range(8):
        m = dict(shared)
        xc = np.ascontiguousarray(
            x[c].reshape(NT, 128, D).transpose(1, 0, 2)).reshape(128, NT * D)
        m["x"] = xc
        in_maps.append(m)
    nc = build()
    res = run_bass_kernel_spmd(nc, in_maps, core_ids=list(range(8)))
    outs = []
    for c in range(8):
        o = np.asarray(res.results[c]["out"])
        outs.append(o.reshape(128, NT, D).transpose(1, 0, 2).reshape(S_TOK, D))
    return np.stack(outs, axis=0).astype(np.float32)


# version 08
# speedup vs baseline: 1.2208x; 1.2208x over previous
import numpy as np
import ml_dtypes

import concourse.bass as bass
import concourse.mybir as mybir
from concourse.bass_utils import run_bass_kernel_spmd

F32 = mybir.dt.float32
BF16 = mybir.dt.bfloat16
AF = mybir.ActivationFunctionType
ALU = mybir.AluOpType
AX = mybir.AxisListType

S_TOK = 2048
D = 1024
H = 16
E = 8
NT = 16
NO = 8
NI = 8
NKT = 16
NF = 32
CSHIFT = 16.0
EPS = 1e-5
VW = H * 65


class Ctr:
    def __init__(self, nc, name):
        self.guard = nc.semaphore(name)
        self.h = self.guard.__enter__()
        self.n = 0

    def inc(self, instr, k=1):
        instr.then_inc(self.h, k)
        self.n += k
        return self.n


def build(stop_after=None, dbg=None):
    nc = bass.Bass("TRN2", target_bir_lowering=False)

    xin = nc.dram_tensor("x", [128, NT * D], F32, kind="ExternalInput")
    wqd = nc.dram_tensor("wqT", [128, 8192], F32, kind="ExternalInput")
    wkd = nc.dram_tensor("wkT", [128, 8192], F32, kind="ExternalInput")
    wvd = nc.dram_tensor("wvT", [128, 8192], F32, kind="ExternalInput")
    wod = nc.dram_tensor("woT", [128, 8192], F32, kind="ExternalInput")
    bqd = nc.dram_tensor("bq", [128, 8], F32, kind="ExternalInput")
    bkd = nc.dram_tensor("bk", [128, 8], F32, kind="ExternalInput")
    bod = nc.dram_tensor("bo", [128, 8], F32, kind="ExternalInput")
    bvd = nc.dram_tensor("bv_row", [1, D], F32, kind="ExternalInput")
    gwd = nc.dram_tensor("gwT", [128, 64], F32, kind="ExternalInput")
    w1d = nc.dram_tensor("w1h", [E, 128, 32768], F32, kind="ExternalInput")
    w2d = nc.dram_tensor("w2h", [E, 128, 32768], F32, kind="ExternalInput")
    cosd = nc.dram_tensor("cos2", [128, S_TOK], F32, kind="ExternalInput")
    sind = nc.dram_tensor("sin2", [128, S_TOK], F32, kind="ExternalInput")
    identbd = nc.dram_tensor("identb", [128, 128], BF16, kind="ExternalInput")
    identfd = nc.dram_tensor("identf", [128, 128], F32, kind="ExternalInput")
    ptd = nc.dram_tensor("PT", [128, 128], BF16, kind="ExternalInput")
    outd = nc.dram_tensor("out", [128, NT * D], F32, kind="ExternalOutput")

    def sball(name, shape, dt, side=None):
        g = nc.sbuf_tensor(name + "_sb", shape, dt, side=side)
        tns = g.__enter__()
        return tns, g

    def psall(name, shape, dt):
        g = nc.psum_tensor(name, shape, dt)
        tns = g.__enter__()
        return tns, g

    def free(*guards):
        for g in guards:
            g.__exit__(None, None, None)

    t, v, s, g_, sy = nc.tensor, nc.vector, nc.scalar, nc.gpsimd, nc.sync
    dxc = Ctr(nc, "dxc")
    dxx = Ctr(nc, "dxx")
    dxo = Ctr(nc, "dxo")
    dxr = Ctr(nc, "dxr")
    dwg = Ctr(nc, "dwg")
    dwa = Ctr(nc, "dwa")
    dwo = Ctr(nc, "dwo")
    dw1s = [Ctr(nc, "dw1a"), Ctr(nc, "dw1b")]
    dw2s = [Ctr(nc, "dw2a"), Ctr(nc, "dw2b"), Ctr(nc, "dw2c")]
    ts = Ctr(nc, "ts")
    vs = Ctr(nc, "vs")
    ss = Ctr(nc, "ss")

    seen = {}

    def W(eng, key, ctr, val):
        if val <= 0:
            return
        cur = seen.get((key, id(ctr)), 0)
        if val > cur:
            eng.wait_ge(ctr.h, val)
            seen[(key, id(ctr))] = val

    wt = lambda c, n: W(t, "t", c, n)
    wv_ = lambda c, n: W(v, "v", c, n)
    ws = lambda c, n: W(s, "s", c, n)
    wg = lambda c, n: W(g_, "g", c, n)
    wsy = lambda c, n: W(sy, "sy", c, n)

    identb, _ = sball("identb", [128, 128], BF16)
    identf, _ = sball("identf", [128, 128], F32)
    ptb, _ = sball("ptb", [128, 128], BF16)
    bq, _ = sball("bq", [128, 8], F32)
    bk, _ = sball("bk", [128, 8], F32)
    bo, _ = sball("bo", [128, 8], F32)
    bv_row, _ = sball("bv_row", [1, D], F32)
    gwf, _ = sball("gwf", [128, 64], F32)
    gwhi, _ = sball("gwhi", [128, 64], BF16)
    gwlo, _ = sball("gwlo", [128, 64], BF16)
    onesb, _ = sball("onesb", [128, 128], BF16)
    onesf, _ = sball("onesf", [1, 128], F32)
    csh, _ = sball("csh", [128, 1], F32)
    epsc, _ = sball("epsc", [128, 1], F32)
    combine, _ = sball("combine", [128, 128], F32)
    bvb, _ = sball("bvb", [128, D], F32)
    sums, _ = sball("sums", [128, NT], F32)
    sumsq, _ = sball("sumsq", [128, NT], F32)
    meanc, _ = sball("meanc", [128, NT], F32)
    negv, _ = sball("negv", [128, NT], F32)
    lnvv, _ = sball("lnvv", [128, NT], F32)
    rstd, _ = sball("rstd", [128, NT], F32)
    sqscr, _ = sball("sqscr", [128, D], F32)
    ga_eq1, _ = sball("ga_eq1", [128, 8], F32)
    ga_eq2, _ = sball("ga_eq2", [128, 8], F32)
    ga_gm, _ = sball("ga_gm", [128, 8], F32)
    ga_v1, _ = sball("ga_v1", [128, 1], F32)
    ga_v2, _ = sball("ga_v2", [128, 1], F32)
    ga_t1, _ = sball("ga_t1", [128, 1], F32)
    ga_g1, _ = sball("ga_g1", [128, 1], F32)
    ga_g2, _ = sball("ga_g2", [128, 1], F32)
    gate_all, _ = sball("gate_all", [128, 128], F32)


    def dbg_stop(tag, src_ap, waits=()):
        for eng_w, c, n in waits:
            wsy(c, n)
            wg(c, n)
        for c in (dxc, dxx, dxo, dxr, dwg, dwa, dwo, *dw1s, *dw2s):
            if c.n:
                g_.wait_ge(c.h, c.n)
        if src_ap.dtype == F32:
            sy.dma_start(out=outd[0:src_ap.shape[0], 0:src_ap.shape[1]],
                         in_=src_ap).then_inc(dxc.h, 16)
            dxc.n += 16
            sy.wait_ge(dxc.h, dxc.n)
        else:
            g_.dma_start(out=outd[0:src_ap.shape[0], 0:src_ap.shape[1]],
                         in_=src_ap).then_inc(dwg.h, 16)
            dwg.n += 16
            g_.wait_ge(dwg.h, dwg.n)

    for dst, src in ((identb, identbd), (identf, identfd), (ptb, ptd),
                     (bq, bqd), (bk, bkd), (bo, bod), (bv_row, bvd),
                     (gwf, gwd)):
        sy.dma_start(out=dst[:, :], in_=src[:, :]).then_inc(dxc.h, 16)
        dxc.n += 16
    dx_consts = dxc.n
    g_.dma_start(out=gwhi[:, :], in_=gwd[:, :]).then_inc(dwg.h, 16); dwg.n += 16
    gwhi_ld = dwg.n

    v.memset(onesb[:, :], 1.0)
    v.memset(onesf[:, :], 1.0)
    v.memset(sumsq[:, :], 0.0)
    v.memset(epsc[:, :], EPS)
    vs.inc(v.memset(csh[:, :], -CSHIFT))
    memset_done = vs.n
    wv_(dxc, dx_consts)
    wv_(dwg, gwhi_ld)
    vs.inc(v.tensor_tensor(gwlo[:, :], gwf[:, :], gwhi[:, :], ALU.subtract))
    gw_done = vs.n

    pbv, pbv_g = psall("pbv", [128, D], F32)
    wt(dxc, dx_consts)
    wt(vs, memset_done)
    t.matmul(pbv[:, 0:512], onesf[0:1, :], bv_row[0:1, 0:512], start=True, stop=True)
    ts.inc(t.matmul(pbv[:, 512:1024], onesf[0:1, :], bv_row[0:1, 512:1024],
                    start=True, stop=True))
    wv_(ts, ts.n)
    vs.inc(v.tensor_copy(bvb[:, :], pbv[:, :]))
    bvb_done = vs.n
    free(pbv_g)

    ln1T, ln1T_g = sball("ln1T", [128, NI * S_TOK], BF16, side="right")
    cos2, cos2_g = sball("cos2", [128, S_TOK], F32, side="right")
    sin2, sin2_g = sball("sin2", [128, S_TOK], F32, side="right")
    wbufA, wbufA_g = sball("wbufA", [128, 8192], BF16, side="right")
    wbufB, wbufB_g = sball("wbufB", [128, 8192], BF16, side="right")
    wbuf = [wbufA, wbufB]

    sy.dma_start(out=cos2[:, :], in_=cosd[:, :]).then_inc(dxr.h, 16); dxr.n += 16
    sy.dma_start(out=sin2[:, :], in_=sind[:, :]).then_inc(dxr.h, 16); dxr.n += 16
    rope_tab = dxr.n
    g_.dma_start(out=wbuf[0][:, :], in_=wqd[:, :]).then_inc(dwa.h, 16); dwa.n += 16
    g_.dma_start(out=wbuf[1][:, :], in_=wkd[:, :]).then_inc(dwa.h, 16); dwa.n += 16
    wq_ld = wk_ld = dwa.n

    xtmp, xtmp_g = sball("xtmp", [128, NT * D], F32)

    xload = []
    for c4 in range(4):
        sy.dma_start(out=xtmp[:, 4096 * c4:4096 * (c4 + 1)],
                     in_=xin[:, 4096 * c4:4096 * (c4 + 1)]).then_inc(dxx.h, 16)
        dxx.n += 16
        xload.append(dxx.n)

    def emit_ln_tr(src_fn, wait_src, dstT, stage, ptrX,
                   loT=None, lostage=None, lotmpf=None):
        stats_v = [0] * NT
        stats_s = [0] * NT
        stage_free_t = [0, 0]
        stage_free_v = [0, 0]
        lostage_free_t = [0, 0]
        tr_evac = [0, 0]

        def stats(tt):
            wait_src(tt)
            src = src_fn(tt)
            i_sum = vs.inc(v.reduce_sum(sums[:, tt:tt + 1], src, axis=AX.X))
            ws(vs, memset_done)
            wait_src(tt, "s")
            i_sq = ss.inc(s.activation(sqscr[:, :], src, AF.Square,
                                       accum_out=sumsq[:, tt:tt + 1]))
            wv_(vs, i_sum)
            i_mean = vs.inc(v.tensor_scalar(meanc[:, tt:tt + 1],
                                            sums[:, tt:tt + 1],
                                            1.0 / D, None, ALU.mult))
            wv_(ss, i_sq)
            i_t = vs.inc(v.tensor_scalar(negv[:, tt:tt + 1], sumsq[:, tt:tt + 1],
                                         -1.0 / D, None, ALU.mult))
            wv_(vs, i_t)
            i_nv = vs.inc(v.scalar_tensor_tensor(
                negv[:, tt:tt + 1], meanc[:, tt:tt + 1], meanc[:, tt:tt + 1],
                negv[:, tt:tt + 1], ALU.mult, ALU.add))
            stats_v[tt] = i_nv
            ws(vs, i_nv)
            i_ln = ss.inc(s.activation(lnvv[:, tt:tt + 1], negv[:, tt:tt + 1],
                                       AF.Ln, bias=epsc[:, 0:1], scale=-1.0))
            ws(ss, i_ln)
            i_rs = ss.inc(s.activation(rstd[:, tt:tt + 1], lnvv[:, tt:tt + 1],
                                       AF.Exp, bias=0.0, scale=-0.5))
            stats_s[tt] = i_rs

        def finish(tt):
            src = src_fn(tt)
            slot = tt % 2
            wv_(ss, stats_s[tt])
            wv_(vs, stats_v[tt])
            wv_(ts, stage_free_t[slot])
            wv_(vs, stage_free_v[slot])
            i_nm = vs.inc(v.tensor_scalar(
                stage[:, D * slot:D * (slot + 1)], src,
                meanc[:, tt:tt + 1], rstd[:, tt:tt + 1],
                ALU.subtract, ALU.mult))
            wt(vs, i_nm)
            wt(vs, tr_evac[0])
            mm = None
            for dt in range(NI):
                mm = t.transpose(
                    ptrX[:, 128 * dt:128 * (dt + 1)],
                    stage[:, D * slot + 128 * dt:D * slot + 128 * (dt + 1)],
                    identb[:, :])
            trm = ts.inc(mm)
            stage_free_t[slot] = trm
            wv_(ts, trm)
            out_ap = bass.AP(dstT, 128 * tt,
                             [[NI * S_TOK, 128], [S_TOK, NI], [1, 128]])
            in_ap = bass.AP(ptrX, 0, [[2048, 128], [128, NI], [1, 128]])
            tr_evac[0] = vs.inc(v.tensor_copy(out_ap, in_ap))
            if loT is not None:
                i_f = vs.inc(v.tensor_scalar(
                    lotmpf[:, :], src, meanc[:, tt:tt + 1], rstd[:, tt:tt + 1],
                    ALU.subtract, ALU.mult))
                wv_(vs, i_f)
                wv_(ts, lostage_free_t[slot])
                i_lo = vs.inc(v.tensor_tensor(
                    lostage[:, D * slot:D * (slot + 1)], lotmpf[:, :],
                    stage[:, D * slot:D * (slot + 1)], ALU.subtract))
                stage_free_v[slot] = i_lo
                wt(vs, i_lo)
                wt(vs, tr_evac[1])
                mm = None
                for dt in range(NI):
                    mm = t.transpose(
                        ptrX[:, 1024 + 128 * dt:1024 + 128 * (dt + 1)],
                        lostage[:, D * slot + 128 * dt:D * slot + 128 * (dt + 1)],
                        identb[:, :])
                trl = ts.inc(mm)
                lostage_free_t[slot] = trl
                wv_(ts, trl)
                out_ap = bass.AP(loT, 128 * tt,
                                 [[NI * S_TOK, 128], [S_TOK, NI], [1, 128]])
                in_ap = bass.AP(ptrX, 1024, [[2048, 128], [128, NI], [1, 128]])
                tr_evac[1] = vs.inc(v.tensor_copy(out_ap, in_ap))

        for tt in range(NT):
            stats(tt)
            if tt > 0:
                finish(tt - 1)
        finish(NT - 1)
        return vs.n

    def wait_x1(tt, eng="v"):
        c = xload[3]
        if eng == "v":
            wv_(dxx, c)
        else:
            ws(dxx, c)

    stage1, stage1_g = sball("stage1", [128, 2 * D], BF16, side="right")
    ptr1, ptr1_g = psall("ptr1", [128, 2048], BF16)
    ln1T_done = emit_ln_tr(lambda tt: xtmp[:, D * tt:D * (tt + 1)], wait_x1,
                           ln1T, stage1, ptr1)
    if stop_after == "ln1":
        dbg_stop("ln1", ln1T[:, :], waits=((wsy, vs, ln1T_done),))
        return nc
    free(ptr1_g, stage1_g, xtmp_g)

    QTr, QTr_g = sball("QTr", [128, NO * S_TOK], BF16)
    KTr, KTr_g = sball("KTr", [128, NO * S_TOK], BF16)
    Vn, Vn_g = sball("Vn", [128, NT * D], BF16)
    qtmp0, qtmp0_g = sball("qtmp0", [128, S_TOK], BF16, side="right")
    qtmp1, qtmp1_g = sball("qtmp1", [128, S_TOK], BF16, side="right")
    qtmps = [qtmp0, qtmp1]
    ctmp, ctmp_g = sball("ctmp", [128, S_TOK], BF16, side="right")
    pqk0, pqk0_g = psall("pqk0", [128, S_TOK], F32)
    pqk1, pqk1_g = psall("pqk1", [128, S_TOK], F32)
    pqks = [pqk0, pqk1]


    def emit_qk(wsel, wld, bias_sb, dstT, init_marks=None):
        marks = init_marks if init_marks is not None else [(0, 0), (0, 0)]
        done = e1 = 0
        for ot in range(NO):
            p = ot % 2
            pqk = pqks[p]
            qtmp = qtmps[p]
            wt(dwa, wld)
            wt(vs, ln1T_done)
            pe1, pm2 = marks[p]
            wt(ss, pe1)
            wt(vs, pm2)
            mm = None
            for i in range(NI):
                lhsT = wbuf[wsel][:, 1024 * i + 128 * ot:1024 * i + 128 * (ot + 1)]
                for qc in range(4):
                    mm = t.matmul(
                        pqk[:, 512 * qc:512 * (qc + 1)], lhsT,
                        ln1T[:, S_TOK * i + 512 * qc:S_TOK * i + 512 * (qc + 1)],
                        start=(i == 0), stop=(i == NI - 1))
            proj = ts.inc(mm)
            ws(ts, proj)
            ws(dxc, dx_consts)
            e1 = ss.inc(s.activation(qtmp[:, :], pqk[:, :], AF.Identity,
                                     bias=bias_sb[:, ot:ot + 1], scale=1.0))
            wt(ss, e1)
            mm = None
            for qc in range(4):
                mm = t.matmul(pqk[:, 512 * qc:512 * (qc + 1)], ptb[:, :],
                              qtmp[:, 512 * qc:512 * (qc + 1)],
                              start=True, stop=True)
            rotmm = ts.inc(mm)
            wv_(ss, e1)
            wv_(dxr, rope_tab)
            e2 = vs.inc(v.tensor_tensor(ctmp[:, :], qtmp[:, :], cos2[:, :],
                                        ALU.mult))
            wv_(ts, rotmm)
            wv_(vs, e2)
            m1 = vs.inc(v.tensor_tensor(pqk[:, :], pqk[:, :], sin2[:, :],
                                        ALU.mult))
            wv_(vs, m1)
            m2 = vs.inc(v.tensor_tensor(dstT[:, S_TOK * ot:S_TOK * (ot + 1)],
                                        ctmp[:, :], pqk[:, :], ALU.add))
            marks[p] = (e1, m2)
            done = m2
        return done, e1, marks

    qdone, q_e1, qk_marks = emit_qk(0, wq_ld, bq, QTr)
    if stop_after == "q":
        dbg_stop("q", QTr[:, :], waits=((None, vs, qdone),))
        return nc
    wg(ts, ts.n)
    g_.dma_start(out=wbuf[0][:, :], in_=wvd[:, :]).then_inc(dwa.h, 16); dwa.n += 16
    wv_ld = dwa.n
    kdone, k_e1, qk_marks = emit_qk(1, wk_ld, bk, KTr, init_marks=qk_marks)
    if stop_after == "k":
        dbg_stop("k", KTr[:, :], waits=((None, vs, kdone),))
        return nc
    free(pqk1_g, pqk0_g)

    pv0, pv0_g = psall("pv0", [128, D], F32)
    pv1, pv1_g = psall("pv1", [128, D], F32)
    pv = [pv0, pv1]
    pv_read = [0, 0]
    for tt in range(NT):
        ping = tt % 2
        wt(dwa, wv_ld)
        wt(vs, ln1T_done)
        wt(vs, kdone)
        wt(ss, k_e1)
        wt(vs, pv_read[ping])
        mm = None
        for i in range(NI):
            lhsT = ln1T[:, S_TOK * i + 128 * tt:S_TOK * i + 128 * (tt + 1)]
            for oc in range(2):
                mm = t.matmul(
                    pv[ping][:, 512 * oc:512 * (oc + 1)], lhsT,
                    wbuf[0][:, 1024 * i + 512 * oc:1024 * i + 512 * (oc + 1)],
                    start=(i == 0), stop=(i == NI - 1))
        vmm = ts.inc(mm)
        wv_(ts, vmm)
        wv_(vs, bvb_done)
        pv_read[ping] = vs.inc(v.tensor_tensor(
            Vn[:, D * tt:D * (tt + 1)], pv[ping][:, :], bvb[:, :], ALU.add))
    v_done = vs.n
    if stop_after == "qkv":
        if dbg == "QTr":
            dbg_stop("qkv", QTr[:, :], waits=((wsy, vs, qdone),))
        elif dbg == "KTr":
            dbg_stop("qkv", KTr[:, :], waits=((wsy, vs, kdone),))
        else:
            dbg_stop("qkv", Vn[:, :], waits=((wsy, vs, v_done),))
        return nc
    free(pv1_g, pv0_g)
    free(ctmp_g, qtmp1_g, qtmp0_g, wbufB_g, wbufA_g, sin2_g, cos2_g, ln1T_g)

    wosbuf, wosbuf_g = sball("wosbuf", [128, 8192], BF16, side="right")
    OTsb, OTsb_g = sball("OTsb", [128, NO * S_TOK], BF16, side="right")
    expbs_g = []
    expbs = []
    for i_ in range(3):
        tns, gg = sball(f"expb{i_}", [128, 1024], BF16, side="right")
        expbs.append(tns)
        expbs_g.append(gg)
    denl, denl_g = sball("denl", [128, 512], F32, side="right")
    invd, invd_g = sball("invd", [128, 512], BF16, side="right")

    wg(ts, ts.n)
    g_.dma_start(out=wosbuf[:, :], in_=wod[:, :]).then_inc(dwo.h, 16); dwo.n += 16
    wo_ld = dwo.n

    srot_g = []
    srot = []
    for i_ in range(3):
        tns, gg = psall(f"pSr{i_}", [128, 1024], F32)
        srot.append(tns)
        srot_g.append(gg)
    pOTp = []
    pOT_g = []
    for i_ in range(1):
        tns, gg = psall(f"pOT{i_}", [128, 512], F32)
        pOTp.append(tns)
        pOT_g.append(gg)
    pDnp = []
    pDn_g = []
    for i_ in range(1):
        tns, gg = psall(f"pDn{i_}", [128, 512], F32)
        pDnp.append(tns)
        pDn_g.append(gg)

    s_read = [0, 0, 0]
    eb_free = [0, 0, 0]
    ot_free = [0]
    dn_free = [0]
    it_idx = 0

    for og in range(NO):
        for qq in range(4):
            qoff = S_TOK * og + 512 * qq
            ping = 0
            sc_marks = {}

            def emit_S(kt):
                sl = kt % 3
                wt(ss, s_read[sl])
                if it_idx == 0 and kt == 0:
                    wt(vs, qdone)
                    wt(vs, kdone)
                    wt(vs, v_done)
                mm = None
                for b in range(2):
                    mm = t.matmul(
                        srot[sl][:, 512 * b:512 * (b + 1)],
                        KTr[64 * b:64 * (b + 1),
                            S_TOK * og + 128 * kt:S_TOK * og + 128 * (kt + 1)],
                        QTr[64 * b:64 * (b + 1), qoff:qoff + 512],
                        start=True, stop=True,
                        tile_position=(64 * b, 0))
                sc_marks[kt] = ts.inc(mm)

            emit_S(0)
            emit_S(1)
            for kt in range(NKT):
                sl = kt % 3
                ws(ts, sc_marks[kt])
                ws(vs, memset_done)
                ws(ts, eb_free[sl])
                ee = ss.inc(s.activation(expbs[sl][:, :], srot[sl][:, :],
                                         AF.Exp, bias=csh[:, 0:1], scale=0.125))
                s_read[sl] = ee
                if kt + 2 < NKT:
                    emit_S(kt + 2)
                wt(ss, ee)
                if kt == 0:
                    wt(vs, ot_free[ping])
                    wt(ss, dn_free[ping])
                for b in range(2):
                    mm = t.matmul(
                        pOTp[ping][64 * b:64 * (b + 1), :],
                        Vn[:, D * kt + 128 * og + 64 * b:
                           D * kt + 128 * og + 64 * (b + 1)],
                        expbs[sl][:, 512 * b:512 * (b + 1)],
                        start=(kt == 0), stop=(kt == NKT - 1),
                        tile_position=(0, 64 * b))
                    ts.inc(mm)
                    mm = t.matmul(
                        pDnp[ping][64 * b:64 * (b + 1), :],
                        onesb[:, 0:64],
                        expbs[sl][:, 512 * b:512 * (b + 1)],
                        start=(kt == 0), stop=(kt == NKT - 1),
                        tile_position=(0, 64 * b))
                    eb_free[sl] = ts.inc(mm)
            av_last = ts.n
            ws(ts, av_last)
            dn1 = ss.inc(s.activation(denl[:, :], pDnp[ping][:, :], AF.Ln,
                                      bias=0.0, scale=1.0))
            dn_free[ping] = dn1
            ws(ss, dn1)
            ws(vs, ot_free[0])
            dn2 = ss.inc(s.activation(invd[:, :], denl[:, :], AF.Exp,
                                      bias=0.0, scale=-1.0))
            wv_(ss, dn2)
            wv_(ts, av_last)
            ot_free[ping] = vs.inc(v.tensor_tensor(
                OTsb[:, qoff:qoff + 512], pOTp[ping][:, :], invd[:, :],
                ALU.mult))
            it_idx += 1
    scores_v = vs.n
    scores_dx = dxo.n
    if stop_after == "scores":
        dbg_stop("scores", OTsb[:, :], waits=((wsy, vs, scores_v), (wsy, dxo, scores_dx)))
        return nc
    free(pDn_g[0], pOT_g[0], srot_g[2], srot_g[1], srot_g[0])
    free(Vn_g, KTr_g, QTr_g)

    x2, x2_g = sball("x2", [128, NT * D], F32)
    attnT, attnT_g = sball("attnT", [128, NO * S_TOK], BF16)

    scores_ts = ts.n
    xfload = []
    wsy(ts, scores_ts)
    for c4 in range(4):
        sy.dma_start(out=x2[:, 4096 * c4:4096 * (c4 + 1)],
                     in_=xin[:, 4096 * c4:4096 * (c4 + 1)]).then_inc(dxx.h, 16)
        dxx.n += 16
        xfload.append(dxx.n)

    po0, po0_g = psall("po0", [128, S_TOK], F32)
    po1, po1_g = psall("po1", [128, S_TOK], F32)
    po = [po0, po1]
    po_read = [0, 0]
    attn_evac = [0] * NO
    for dt in range(NO):
        ping = dt % 2
        wt(dwo, wo_ld)
        wt(vs, scores_v)
        wt(dxo, scores_dx)
        wt(ss, ss.n if dt == 0 else 0)
        wt(ss, po_read[ping])
        mm = None
        for j in range(NO):
            lhsT = wosbuf[:, 1024 * j + 128 * dt:1024 * j + 128 * (dt + 1)]
            for qc in range(4):
                mm = t.matmul(
                    po[ping][:, 512 * qc:512 * (qc + 1)], lhsT,
                    OTsb[:, S_TOK * j + 512 * qc:S_TOK * j + 512 * (qc + 1)],
                    start=(j == 0), stop=(j == NO - 1))
        omm = ts.inc(mm)
        ws(ts, omm)
        po_read[ping] = ss.inc(s.activation(
            attnT[:, S_TOK * dt:S_TOK * (dt + 1)], po[ping][:, :], AF.Identity,
            bias=bo[:, dt:dt + 1], scale=1.0))
        attn_evac[dt] = po_read[ping]
    free(po1_g, po0_g)

    ptr2, ptr2_g = psall("ptr2", [128, 2048], BF16)
    res_marks = [0] * NT
    tr2_evac = [0, 0]
    for tt in range(NT):
        ping = tt % 2
        wt(ss, attn_evac[NO - 1])
        wt(vs, tr2_evac[ping])
        mm = None
        for dt in range(NO):
            mm = t.transpose(
                ptr2[:, 1024 * ping + 128 * dt:1024 * ping + 128 * (dt + 1)],
                attnT[:, S_TOK * dt + 128 * tt:S_TOK * dt + 128 * (tt + 1)],
                identb[:, :])
        trm = ts.inc(mm)
        wv_(ts, trm)
        wv_(dxx, xfload[3])
        in_ap = bass.AP(ptr2, 1024 * ping, [[2048, 128], [128, NI], [1, 128]])
        tr2_evac[ping] = vs.inc(v.tensor_tensor(
            x2[:, D * tt:D * (tt + 1)], x2[:, D * tt:D * (tt + 1)],
            in_ap, ALU.add))
        res_marks[tt] = tr2_evac[ping]
    if stop_after == "oproj":
        dbg_stop("oproj", x2[:, :], waits=((wsy, vs, res_marks[NT - 1]),))
        return nc
    p4_ts_end = ts.n
    free(ptr2_g, attnT_g)
    free(invd_g, denl_g, expbs_g[2], expbs_g[1], expbs_g[0], OTsb_g, wosbuf_g)

    ln2T, ln2T_g = sball("ln2T", [128, NI * S_TOK], BF16, side="right")
    ln2loT, ln2loT_g = sball("ln2loT", [128, NI * S_TOK], BF16, side="right")
    stage2, stage2_g = sball("stage2", [128, 2 * D], BF16, side="right")
    lostage, lostage_g = sball("lostage", [128, 2 * D], BF16, side="right")
    lotmpf, lotmpf_g = sball("lotmpf", [128, D], F32, side="right")
    gateTs, gateTs_g = sball("gateTs", [8, S_TOK], F32, side="right")

    wv_(ss, ss.n)
    ms2 = vs.inc(v.memset(sumsq[:, :], 0.0))

    def wait_x2(tt, eng="v"):
        c = res_marks[tt]
        if eng == "v":
            wv_(vs, c)
            wv_(vs, ms2)
        else:
            ws(vs, c)
            ws(vs, ms2)

    ptr3, ptr3_g = psall("ptr3", [128, 2048], BF16)
    ln2T_done = emit_ln_tr(lambda tt: x2[:, D * tt:D * (tt + 1)], wait_x2,
                           ln2T, stage2, ptr3,
                           loT=ln2loT, lostage=lostage, lotmpf=lotmpf)

    pgT, pgT_g = psall("pgT", [8, S_TOK], F32)
    wt(vs, ln2T_done)
    wt(vs, gw_done)
    wt(dwg, gwhi_ld)
    mm = None
    combos = [(gwhi, ln2T), (gwhi, ln2loT), (gwlo, ln2T), (gwlo, ln2loT)]
    ncomb = len(combos)
    for i in range(NI):
        for ci, (gwx, lnx) in enumerate(combos):
            first = (i == 0 and ci == 0)
            last = (i == NI - 1 and ci == ncomb - 1)
            for tc in range(4):
                mm = t.matmul(
                    pgT[:, 512 * tc:512 * (tc + 1)], gwx[:, 8 * i:8 * (i + 1)],
                    lnx[:, S_TOK * i + 512 * tc:S_TOK * i + 512 * (tc + 1)],
                    start=first, stop=last)
    gmm = ts.inc(mm)
    ws(ts, gmm)
    gevac = ss.inc(s.copy(gateTs[:, :], pgT[:, :]))
    free(pgT_g)

    pgt, pgt_g = psall("pgt", [128, 128], F32)
    wt(ss, gevac)
    mm = None
    for tt in range(NT):
        mm = t.transpose(pgt[:, 8 * tt:8 * (tt + 1)],
                         gateTs[0:8, 128 * tt:128 * (tt + 1)], identf[0:8, 0:8])
    gtr = ts.inc(mm)
    wv_(ts, gtr)
    gaev = vs.inc(v.tensor_copy(gate_all[:, :], pgt[:, :]))
    free(pgt_g)

    comb_marks = [0] * NT
    for tt in range(NT):
        ga = gate_all[:, 8 * tt:8 * (tt + 1)]
        cb = combine[:, 8 * tt:8 * (tt + 1)]
        wv_(vs, gaev)
        i1 = vs.inc(v.reduce_max(ga_v1[:, 0:1], ga, axis=AX.X))
        wv_(vs, i1)
        i2 = vs.inc(v.tensor_scalar(ga_eq1[:, :], ga, ga_v1[:, 0:1], None,
                                    ALU.is_equal))
        wv_(vs, i2)
        i3 = vs.inc(v.scalar_tensor_tensor(ga_gm[:, :], ga_eq1[:, :], -1e9,
                                           ga, ALU.mult, ALU.add))
        wv_(vs, i3)
        i4 = vs.inc(v.reduce_max(ga_v2[:, 0:1], ga_gm[:, :], axis=AX.X))
        wv_(vs, i4)
        i5 = vs.inc(v.tensor_scalar(ga_eq2[:, :], ga_gm[:, :], ga_v2[:, 0:1],
                                    None, ALU.is_equal))
        i6 = vs.inc(v.tensor_scalar(ga_t1[:, 0:1], ga_v2[:, 0:1], ga_v1[:, 0:1],
                                    None, ALU.subtract))
        ws(vs, i6)
        i7 = ss.inc(s.activation(ga_t1[:, 0:1], ga_t1[:, 0:1], AF.Exp,
                                 bias=0.0, scale=1.0))
        wv_(ss, i7)
        i8 = vs.inc(v.tensor_scalar(ga_t1[:, 0:1], ga_t1[:, 0:1], 1.0, None,
                                    ALU.add))
        wv_(vs, i8)
        i9 = vs.inc(v.reciprocal(ga_g1[:, 0:1], ga_t1[:, 0:1]))
        wv_(vs, i9)
        i10 = vs.inc(v.tensor_scalar(ga_g2[:, 0:1], ga_g1[:, 0:1], -1.0, 1.0,
                                     ALU.mult, ALU.add))
        wv_(vs, i10)
        i11 = vs.inc(v.tensor_scalar(cb, ga_eq1[:, :], ga_g1[:, 0:1], None,
                                     ALU.mult))
        wv_(vs, i11)
        comb_marks[tt] = vs.inc(v.scalar_tensor_tensor(
            cb, ga_eq2[:, :], ga_g2[:, 0:1], cb, ALU.mult, ALU.add))
    comb_done = vs.n
    if stop_after == "ln2":
        if dbg == "comb":
            dbg_stop("ln2", combine[:, :], waits=((wsy, vs, comb_done),))
        else:
            dbg_stop("ln2", ln2T[:, :], waits=((wsy, vs, ln2T_done), (wsy, vs, comb_done)))
        return nc
    free(gateTs_g, lotmpf_g, lostage_g, stage2_g, ln2loT_g, ptr3_g)

    hTb, hTb_g = sball("hTb", [128, NF * 1024], BF16)
    w1r0, w1r0_g = sball("w1r0", [128, 4096], BF16)
    w1r1, w1r1_g = sball("w1r1", [128, 4096], BF16)
    w1r = [w1r0, w1r1]
    w2r0, w2r0_g = sball("w2r0", [128, 2048], BF16)
    w2r1, w2r1_g = sball("w2r1", [128, 2048], BF16)
    w2r2, w2r2_g = sball("w2r2", [128, 2048], BF16)
    w2r = [w2r0, w2r1, w2r2]
    parena, parena_g = psall("parena", [128, 4096], F32)

    N_W1C = 8
    N_W2C = 16
    w1_loaded = {}
    w2_loaded = {}
    w1_cons = {}
    w2_cons = {}
    yevac = [[0] * 8 for _ in range(2)]
    prev_yevac = [0] * 8

    wt(vs, comb_done)
    wt(ss, gevac)
    moe_dma_safe = ts.n
    NTG = 2 * E

    def emit_w1dma(kc):
        if kc >= NTG * N_W1C:
            return
        e_, c_ = kc // N_W1C // 2, kc % N_W1C
        if kc >= 2:
            wg(ts, w1_cons[kc - 2])
        else:
            wg(ts, moe_dma_safe)
        sl = dw1s[kc % 2]
        g_.dma_start(out=w1r[kc % 2][:, :],
                     in_=w1d[e_, :, 4096 * c_:4096 * (c_ + 1)]
                     ).then_inc(sl.h, 16)
        sl.n += 16
        w1_loaded[kc] = sl.n

    def emit_w2dma(kc):
        if kc >= NTG * N_W2C:
            return
        e_, c_ = kc // N_W2C // 2, kc % N_W2C
        if kc >= 3:
            wg(ts, w2_cons[kc - 3])
        else:
            wg(ts, moe_dma_safe)
        sl = dw2s[kc % 3]
        g_.dma_start(out=w2r[kc % 3][:, :],
                     in_=w2d[e_, :, 2048 * c_:2048 * (c_ + 1)]
                     ).then_inc(sl.h, 16)
        sl.n += 16
        w2_loaded[kc] = sl.n

    for kc in range(2):
        emit_w1dma(kc)
    for kc in range(3):
        emit_w2dma(kc)

    for e in range(E):
        for tg in range(2):
            key = (e, tg)
            relu_marks = [0] * NF
            for ft in range(NF):
                ping = ft % 2
                kc = (e * 2 + tg) * N_W1C + ft // 4
                wt(dw1s[kc % 2], w1_loaded[kc])
                wt(vs, ln2T_done)
                if ft >= 2:
                    wt(ss, relu_marks[ft - 2])
                else:
                    wt(vs, prev_yevac[2 * ping])
                    wt(vs, prev_yevac[2 * ping + 1])
                mm = None
                for i in range(NI):
                    lhsT = w1r[kc % 2][:, 1024 * (ft % 4) + 128 * i:
                                       1024 * (ft % 4) + 128 * (i + 1)]
                    for hf in range(2):
                        mm = t.matmul(
                            parena[:, 1024 * ping + 512 * hf:1024 * ping + 512 * (hf + 1)],
                            lhsT,
                            ln2T[:, S_TOK * i + 1024 * tg + 512 * hf:
                                 S_TOK * i + 1024 * tg + 512 * (hf + 1)],
                            start=(i == 0), stop=(i == NI - 1))
                m1 = ts.inc(mm)
                if ft % 4 == 3:
                    w1_cons[kc] = m1
                    emit_w1dma(kc + 2)
                ws(ts, m1)
                relu_marks[ft] = ss.inc(s.activation(
                    hTb[:, 1024 * ft:1024 * (ft + 1)],
                    parena[:, 1024 * ping:1024 * (ping + 1)], AF.Relu))
            relu_all = ss.n

            for dc in range(2):
                for ft in range(NF):
                    kc = (e * 2 + tg) * N_W2C + dc * 8 + ft // 4
                    wt(dw2s[kc % 3], w2_loaded[kc])
                    wt(ss, relu_marks[ft])
                    for tt in range(8):
                        if ft == 0:
                            if dc == 0:
                                if tt >= 4:
                                    wt(vs, prev_yevac[tt])
                                else:
                                    wt(ss, relu_marks[NF - 2])
                                    wt(ss, relu_marks[NF - 1])
                            else:
                                wt(vs, yevac[0][tt])
                        mm = t.matmul(
                            parena[:, 512 * tt:512 * (tt + 1)],
                            hTb[:, 1024 * ft + 128 * tt:1024 * ft + 128 * (tt + 1)],
                            w2r[kc % 3][:, 512 * (ft % 4):512 * (ft % 4) + 512],
                            start=(ft == 0), stop=(ft == NF - 1))
                    m2 = ts.inc(mm)
                    if ft % 4 == 3:
                        w2_cons[kc] = m2
                        emit_w2dma(kc + 3)
                    if ft == NF - 1:
                        mm2_stop = m2
                for tt in range(8):
                    tgl = 8 * tg + tt
                    wv_(ts, mm2_stop)
                    wv_(vs, comb_done)
                    wv_(vs, res_marks[tgl])
                    yevac[dc][tt] = vs.inc(v.scalar_tensor_tensor(
                        x2[:, D * tgl + 512 * dc:D * tgl + 512 * (dc + 1)],
                        parena[:, 512 * tt:512 * (tt + 1)],
                        combine[:, 8 * tgl + e:8 * tgl + e + 1],
                        x2[:, D * tgl + 512 * dc:D * tgl + 512 * (dc + 1)],
                        ALU.mult, ALU.add))
            prev_yevac = list(yevac[1])
    moe_done = vs.n

    for c4 in range(4):
        wsy(vs, moe_done)
        sy.dma_start(out=outd[:, 4096 * c4:4096 * (c4 + 1)],
                     in_=x2[:, 4096 * c4:4096 * (c4 + 1)]).then_inc(dxc.h, 16)
        dxc.n += 16
    sy.wait_ge(dxc.h, dxc.n)

    free(parena_g, w2r2_g, w2r1_g, w2r0_g, w1r1_g, w1r0_g, hTb_g)
    free(ln2T_g)
    free(x2_g)
    return nc


def _host_layouts(inputs):
    ins = {k: np.asarray(v) for k, v in inputs.items()}

    def wlay(w):
        wT = np.ascontiguousarray(ins[w].T)
        r = wT.reshape(8, 128, wT.shape[1])
        return np.ascontiguousarray(r.transpose(1, 0, 2)).reshape(128, -1)

    maps = {}
    maps["wqT"] = wlay("L0_attn_wq")
    maps["wkT"] = wlay("L0_attn_wk")
    maps["wvT"] = wlay("L0_attn_wv")
    maps["woT"] = wlay("L0_attn_wo")
    maps["bq"] = np.ascontiguousarray(ins["L0_attn_bq"].reshape(8, 128).T)
    maps["bk"] = np.ascontiguousarray(ins["L0_attn_bk"].reshape(8, 128).T)
    maps["bo"] = np.ascontiguousarray(ins["L0_attn_bo"].reshape(8, 128).T)
    maps["bv_row"] = np.ascontiguousarray(ins["L0_attn_bv"].reshape(1, D))
    gwT = np.ascontiguousarray(ins["L0_ffn_gate_w"].T)
    maps["gwT"] = np.ascontiguousarray(
        gwT.reshape(8, 128, 8).transpose(1, 0, 2)).reshape(128, 64)

    w1 = ins["L0_ffn_w1"]
    w1h = np.empty((E, 128, 32768), np.float32)
    for e in range(E):
        a = w1[e].reshape(32, 128, 8, 128)
        w1h[e] = np.ascontiguousarray(a.transpose(3, 0, 2, 1)).reshape(128, 32768)
    maps["w1h"] = w1h
    w2 = ins["L0_ffn_w2"]
    w2h = np.empty((E, 128, 32768), np.float32)
    for e in range(E):
        b = w2[e].reshape(2, 512, 32, 128)
        w2h[e] = np.ascontiguousarray(b.transpose(3, 0, 2, 1)).reshape(128, 32768)
    maps["w2h"] = w2h

    j = np.arange(32, dtype=np.float64)
    freqs = ROPE_BASE_F ** (2.0 * j / 64.0)
    tpos = np.arange(S_TOK, dtype=np.float64)
    ang = tpos[None, :] / freqs[:, None]
    cos1 = np.repeat(np.cos(ang), 2, axis=0)
    sin1 = np.repeat(np.sin(ang), 2, axis=0)
    maps["cos2"] = np.ascontiguousarray(
        np.concatenate([cos1, cos1], axis=0)).astype(np.float32)
    maps["sin2"] = np.ascontiguousarray(
        np.concatenate([sin1, sin1], axis=0)).astype(np.float32)

    maps["identb"] = np.eye(128, dtype=ml_dtypes.bfloat16)
    maps["identf"] = np.eye(128, dtype=np.float32)
    pt = np.zeros((128, 128), np.float32)
    for jj in range(64):
        pt[2 * jj + 1, 2 * jj] = -1.0
        pt[2 * jj, 2 * jj + 1] = 1.0
    maps["PT"] = pt.astype(ml_dtypes.bfloat16)
    return maps


ROPE_BASE_F = 10000.0


def kernel(**inputs):
    shared = _host_layouts(inputs)
    x = np.asarray(inputs["L0_attn_x"])
    in_maps = []
    for c in range(8):
        m = dict(shared)
        xc = np.ascontiguousarray(
            x[c].reshape(NT, 128, D).transpose(1, 0, 2)).reshape(128, NT * D)
        m["x"] = xc
        in_maps.append(m)
    nc = build()
    res = run_bass_kernel_spmd(nc, in_maps, core_ids=list(range(8)))
    outs = []
    for c in range(8):
        o = np.asarray(res.results[c]["out"])
        outs.append(o.reshape(128, NT, D).transpose(1, 0, 2).reshape(S_TOK, D))
    return np.stack(outs, axis=0).astype(np.float32)
